# version 14
# baseline (speedup 1.0000x reference)
import numpy as np
import ml_dtypes
import concourse.bass as bass
import concourse.mybir as mybir
from concourse.bass_utils import run_bass_kernel_spmd

F32 = mybir.dt.float32
BF16 = mybir.dt.bfloat16
AF = mybir.ActivationFunctionType
ALU = mybir.AluOpType
AX = mybir.AxisListType.X

NCORES = 8
L = 4096
TS = 512
NT = L // TS
STW = 2
NSTS = NT // STW
D = 1024
DI = 2048
NSTATE = 16
FI = 1408
NI = 11
NKD = 8
NCH = 16
NE = 9
DG = 2
NG = NCH // DG
EPS = 1e-5

ENGS = ("pe", "act", "dve", "pool", "sp")


class _Stop(Exception):
    pass


class Emitter:
    def __init__(self, nc, sems, handles):
        self.nc = nc
        self.sems = sems
        self.h = handles
        self.cur = None
        self.cnt = {k: 0 for k in sems}

    def begin(self, cur):
        self.cur = cur
        self.cnt = {k: 0 for k in self.sems}

    def ins(self, eng, fn, incs=()):
        inst = fn() if self.cur == eng else None
        for s, n in incs:
            self.cnt[s] += n
            if inst is not None:
                inst.then_inc(self.sems[s], n)

    def wait(self, eng, sem, val):
        if self.cur == eng and val > 0:
            self.h[eng].wait_ge(self.sems[sem], val)

    def waitc(self, eng, sem):
        self.wait(eng, sem, self.cnt[sem])


def build(stage="full", detect_races=True):
    nc = bass.Bass("TRN2", target_bir_lowering=False, debug=False,
                   detect_race_conditions=detect_races)

    def din(name, shape, dt=F32):
        return nc.dram_tensor(name, list(shape), dt, kind="ExternalInput").ap()

    xT_d = din("xT", [D, L])
    wg_d = din("wg_sw", [NE, 128, NKD * FI])
    wu_d = din("wu_sw", [NE, 128, NKD * FI])
    wd_d = din("wd_sw", [NE, 8, 128, NI * 128])
    ip_d = din("ip_sw", [32, 128, NKD * 128])
    ow_d = din("ow_sw", [8, 128, NCH * 128])
    xp_d = din("xp_sw", [128, NCH * 17])
    rw_d = din("rw_sw", [128, NKD * 8])
    cw4_d = din("cw4_sw", [128, NCH * 4])
    dtwc_d = din("dtw_sw", [128, NCH])
    dtb_d = din("dtb_sw", [128, NCH])
    ap_d = din("a_sw", [128, NCH * NSTATE])
    dcol_d = din("d_sw", [128, NCH])
    ob_d = din("ob_sw", [128, 8])
    ones_c = din("ones_col", [128, 1])
    ones_r = din("ones_row", [1, 128])
    idt_d = din("id128", [128, 128])
    oneh8_d = din("oneh8", [8, 8 * 128])

    outT_d = nc.dram_tensor("outT", [D, L // 2], F32, kind="ExternalOutput").ap()

    uspill = nc.dram_tensor("uspill", [DI, L // 2], F32, kind="Internal").ap()
    szspill = nc.dram_tensor("szspill", [DI, L // 2], F32, kind="Internal").ap()
    yspill = nc.dram_tensor("yspill", [DI, L // 2], F32, kind="Internal").ap()
    xaspill = nc.dram_tensor("xaspill", [D, L // 2], F32, kind="Internal").ap()
    bspill = nc.dram_tensor("bspill", [17, L], F32, kind="Internal").ap()

    A = nc.alloc_sbuf_tensor
    xacc = [A("xacc0", [128, NKD * STW * TS], F32)]
    xn = A("xn_t", [128, NKD * STW * TS], BF16)
    cwfm = A("cwfm", [8, STW * TS], BF16)
    wgb = A("wgb", [128, NKD * FI], BF16)
    wub = A("wub", [128, NKD * FI], BF16)
    wdb = [A(f"wdb{i}", [128, NI * 128], BF16) for i in range(2)]
    bigb = A("bigb", [128, 2 * NI * TS], BF16)
    gub = [bigb[:, i * NI * TS:(i + 1) * NI * TS] for i in range(2)]
    ipwb = [A(f"ipwb{i}", [128, NKD * 128], BF16) for i in range(2)]
    owb = [A(f"owb{i}", [128, NCH * 128], BF16) for i in range(2)]
    upw = A("upw", [128, TS + 3], F32)
    ccarry = A("ccarry", [128, NCH * 3], F32)
    ut = A("ut", [128, TS], F32)
    szt = A("szt", [128, TS], F32)
    rld_u = A("rld_u", [128, TS], F32)
    rld_sz = A("rld_sz", [128, TS], F32)
    rld_y = A("rld_y", [128, TS], F32)
    ygt = bigb[:, 0:NCH * TS]
    dw = A("dw", [128, DG * TS], F32)
    brep = A("brep", [128, NSTATE * TS], BF16)
    sa = [A(f"sa{i}", [128, TS], F32) for i in range(2)]
    sdb = [A(f"sdb{i}", [128, TS], F32) for i in range(2)]
    sh = [A(f"sh{i}", [128, TS], F32) for i in range(2)]
    sbr = A("sbr", [128, TS], F32)
    sy = A("sy", [128, DG * TS], F32)
    carry = A("carry", [128, 256], F32)
    xdbl = A("xdbl", [17, TS], F32)
    sq_ln = A("sq_ln", [128, TS], F32)
    sq_sp = A("sq_sp", [128, TS], F32)
    sq_dv = A("sq_dv", [128, TS], F32)
    lnr = A("lnr", [1, 3 * TS], F32)
    scsb = A("scsb", [8, TS], F32)
    sctok = A("sctok", [128, 44], F32)
    cwork = A("cwork", [128, 4 * 8], F32)
    cwork2 = A("cwork2", [128, 4 * 8], F32)
    cwork3 = A("cwork3", [128, 4 * 8], F32)
    ccol = A("ccol", [128, 8], F32)
    cwtk = A("cwtk", [128, 4 * 8], F32)
    kc = A("kc", [1, 4], F32)
    rwf = A("rwf", [128, NKD * 8], F32)
    wsr = A("wsr", [128, 8], F32)
    xpwb = A("xpwb", [128, NCH * 17], F32)
    cw4b = A("cw4b", [128, NCH * 4], F32)
    dtwcolb = A("dtwcolb", [128, NCH], F32)
    dtbb = A("dtbb", [128, NCH], F32)
    acolb = A("acolb", [128, NCH * NSTATE], F32)
    dcolb = A("dcolb", [128, NCH], F32)
    obb = A("obb", [128, 8], F32)
    onecb = A("onecb", [128, 1], F32)
    onerb = A("onerb", [1, 128], F32)
    idtb = A("idtb", [128, 128], F32)
    oneh8b = A("oneh8b", [8, 8 * 128], BF16)

    rem = nc.sbuf_bytes_remaining

    P_ = nc.alloc_psum_tensor
    ps = [P_(f"ps{i}", [128, TS], F32) for i in range(8)]

    sem_names = ["sx", "spq", "rlq", "wgq", "wdq0", "wdq1", "ipq0", "ipq1",
                 "owq0", "owq1", "xdq", "syq", "bq0", "dtq",
                 "pe", "act", "dve", "pool"]

    import contextlib
    stack = contextlib.ExitStack()
    sems = {n: stack.enter_context(nc.semaphore(n)) for n in sem_names}
    block = stack.enter_context(nc.Block())

    def emit_all(E: Emitter, stage=stage):
        nct, nca, ncv, ncp, ncs = nc.tensor, nc.scalar, nc.vector, nc.gpsimd, nc.sync
        TT = ncv.tensor_tensor
        TSC = ncv.tensor_scalar
        STT = ncv.scalar_tensor_tensor

        SYNC_SEMS = ("sx", "spq", "rlq", "syq")

        def dma(q, out, in_):
            eng = "sp" if q in SYNC_SEMS else "pool"
            obj = ncs if q in SYNC_SEMS else ncp
            E.ins(eng, lambda: obj.dma_start(out=out, in_=in_), [(q, 16)])

        def mm(lhsT, rhs, out, start, stop, milestone=False):
            incs = [("pe", 1)] if milestone else ()
            E.ins("pe", lambda: nct.matmul(out, lhsT, rhs, start=start,
                                           stop=stop), incs)

        def act(out, in_, func, milestone=True, **kw):
            incs = [("act", 1)] if milestone else ()
            E.ins("act", lambda: nca.activation(out, in_, func, **kw), incs)

        def dve(fn, milestone=True):
            incs = [("dve", 1)] if milestone else ()
            E.ins("dve", fn, incs)

        def pool(fn, milestone=True):
            incs = [("pool", 1)] if milestone else ()
            E.ins("pool", fn, incs)

        dma("sx", xpwb[:], xp_d)
        dma("sx", cw4b[:], cw4_d)
        dma("sx", dtbb[:], dtb_d)
        dma("sx", acolb[:], ap_d)
        dma("sx", dcolb[:], dcol_d)
        dma("sx", obb[:], ob_d)
        dma("sx", onecb[:], ones_c)
        dma("sx", onerb[:], ones_r)
        dma("sx", idtb[:], idt_d)
        dma("sx", rwf[:], rw_d)
        dma("sx", dtwcolb[:], dtwc_d)
        dma("wgq", oneh8b[:], oneh8_d)
        E.waitc("sp", "sx")
        for eng in ("pe", "act", "dve", "pool"):
            E.waitc(eng, "sx")
            E.waitc(eng, "wgq")
        pool(lambda: ncp.memset(kc[0:1, 0:1], EPS))
        pool(lambda: ncp.memset(kc[0:1, 1:2], -0.5))
        E.waitc("act", "pool")
        E.waitc("dve", "pool")
        act(acolb[:], acolb[:], AF.Exp)
        E.waitc("dve", "act")
        dve(lambda: TSC(acolb[:], acolb[:], -1.0, None, ALU.mult))
        dve(lambda: ncv.memset(ccarry[:], 0.0))
        dve(lambda: ncv.memset(carry[:], 0.0))
        for k in range(NKD):
            mm(onecb[:], rwf[:, k * 8:(k + 1) * 8], ps[5][0:1, 0:8],
               k == 0, k == NKD - 1, milestone=(k == NKD - 1))
        E.waitc("act", "pe")
        act(lnr[0:1, 0:8], ps[5][0:1, 0:8], AF.Copy)
        E.waitc("pe", "act")
        mm(onerb[:], lnr[0:1, 0:8], ps[5][:, 0:8], True, True,
           milestone=True)
        E.waitc("act", "pe")
        act(wsr[:], ps[5][:, 0:8], AF.Copy)

        front_done = {}
        scan_done = {}
        xspill_done = {}
        led_gu_free = [dict(E.cnt)]
        led_cw = [dict(E.cnt)]
        psf = {b: dict(E.cnt) for b in (0, 1, 2, 3, 7)}
        ip_free = [dict(E.cnt), dict(E.cnt)]
        wd_free = [dict(E.cnt), dict(E.cnt)]
        ow_free = [dict(E.cnt), dict(E.cnt)]
        rld_free = [dict(E.cnt), dict(E.cnt)]
        ps4_free = [dict(E.cnt)]
        led56 = [dict(E.cnt)]
        led7 = [dict(E.cnt)]

        def ln_pass(buf, router=False, xw_dve=0):
            for t in range(STW if not stage.startswith("ln_") else 1):
                E.wait("pe", "dve", max(led56[0]["dve"], xw_dve))
                E.wait("pe", "act", led56[0]["act"])
                for ch in range(NKD):
                    xch = xacc[buf][:, ch * STW * TS + t * TS:
                                    ch * STW * TS + (t + 1) * TS]
                    mm(onecb[:], xch, ps[5][0:1, :], ch == 0, ch == NKD - 1,
                       milestone=(ch == NKD - 1))
                for ch in range(NKD):
                    xch = xacc[buf][:, ch * STW * TS + t * TS:
                                    ch * STW * TS + (t + 1) * TS]
                    E.waitc("act", "pe") if ch > 0 else None
                    if ch == 0:
                        E.waitc("act", "dve")
                    act(sq_ln[:], xch, AF.Square)
                    E.waitc("pe", "act")
                    mm(onecb[:], sq_ln[:], ps[6][0:1, :], ch == 0,
                       ch == NKD - 1, milestone=True)
                E.waitc("dve", "pe")
                r_mean = lnr[0:1, 0:TS]
                r_var = lnr[0:1, TS:2 * TS]
                r_mrs = lnr[0:1, 2 * TS:3 * TS]
                dve(lambda: TSC(r_mean, ps[5][0:1, :], 1.0 / D, None,
                                ALU.mult), milestone=False)
                dve(lambda: TSC(r_var, ps[6][0:1, :], 1.0 / D, None,
                                ALU.mult), milestone=False)
                dve(lambda: TT(r_mrs, r_mean, r_mean, ALU.mult),
                    milestone=False)
                dve(lambda: TT(r_var, r_var, r_mrs, ALU.subtract))
                E.waitc("act", "dve")
                act(r_var, r_var, AF.Ln, bias=kc[0:1, 0:1], milestone=False)
                act(r_var, r_var, AF.Exp, scale=kc[0:1, 1:2])
                E.waitc("dve", "act")
                dve(lambda: TT(r_mrs, r_mean, r_var, ALU.mult))
                led56[0] = dict(E.cnt)
                E.wait("pe", "dve", led56[0]["dve"])
                mm(onerb[:], r_var, ps[5][:, :], True, True)
                mm(onerb[:], r_mrs, ps[6][:, :], True, True, milestone=True)
                E.waitc("dve", "pe")
                for ch in range(NKD):
                    xch = xacc[buf][:, ch * STW * TS + t * TS:
                                    ch * STW * TS + (t + 1) * TS]
                    xnch = xn[:, ch * STW * TS + t * TS:
                              ch * STW * TS + (t + 1) * TS]
                    dve(lambda xch=xch: TT(sq_dv[:], xch, ps[5][:, :],
                                           ALU.mult), milestone=False)
                    dve(lambda xnch=xnch: TT(xnch, sq_dv[:], ps[6][:, :],
                                             ALU.subtract),
                        milestone=(ch == NKD - 1))
                led56[0] = dict(E.cnt)
                if router:
                    router_tile(t, buf)

        def router_tile(t, buf):
            E.wait("pe", "dve", led56[0]["dve"])
            E.wait("pe", "act", led56[0]["act"])
            for k in range(NKD):
                xch = xacc[buf][:, k * STW * TS + t * TS:
                                k * STW * TS + (t + 1) * TS]
                mm(rwf[:, k * 8:(k + 1) * 8], xch, ps[5][0:8, :],
                   k == 0, k == NKD - 1, milestone=(k == NKD - 1))
            E.waitc("act", "pe")
            act(scsb[:], ps[5][0:8, :], AF.Copy)
            led56[0] = dict(E.cnt)
            E.waitc("pe", "act")
            r_mean = lnr[0:1, 0:TS]
            r_rstd = lnr[0:1, TS:2 * TS]
            for sub in range(4):
                E.ins("pe", lambda s_=sub: nct.transpose(
                    ps[6][:, s_ * 8:(s_ + 1) * 8],
                    scsb[0:8, s_ * 128:(s_ + 1) * 128], idtb[0:8, 0:8]),
                    [("pe", 1)])
                E.ins("pe", lambda s_=sub: nct.transpose(
                    ps[6][:, 32 + s_:33 + s_],
                    r_mean[0:1, s_ * 128:(s_ + 1) * 128], idtb[0:1, 0:1]),
                    [("pe", 1)])
                E.ins("pe", lambda s_=sub: nct.transpose(
                    ps[6][:, 36 + s_:37 + s_],
                    r_rstd[0:1, s_ * 128:(s_ + 1) * 128], idtb[0:1, 0:1]),
                    [("pe", 1)])
            E.waitc("act", "pe")
            act(sctok[:, 0:40], ps[6][:, 0:40], AF.Copy)
            led56[0] = dict(E.cnt)
            E.waitc("dve", "act")
            for sub in range(4):
                pcol = slice(sub * 8, (sub + 1) * 8)
                mcol = sctok[:, 32 + sub:33 + sub]
                rcol = sctok[:, 36 + sub:37 + sub]
                sto = sctok[:, pcol]
                q = cwork2[:, pcol]
                pb = cwork[:, pcol]
                g1 = cwork3[:, pcol]
                dve(lambda: TSC(pb, wsr[:, :], mcol, None, ALU.mult))
                E.waitc("dve", "dve")
                dve(lambda: TT(q, sto, pb, ALU.subtract))
                E.waitc("dve", "dve")
                dve(lambda: ncv.reduce_max(ccol[:, 0:1], q, axis=AX,
                                           negate=True))
                E.waitc("dve", "dve")
                dve(lambda: TT(ccol[:, 1:2], ccol[:, 0:1], rcol, ALU.mult))
                E.waitc("dve", "dve")
                E.waitc("act", "dve")
                act(pb, q, AF.Exp, scale=rcol, bias=ccol[:, 1:2])
                E.waitc("dve", "act")
                dve(lambda: ncv.reduce_max(ccol[:, 2:3], pb, axis=AX))
                E.waitc("dve", "dve")
                dve(lambda: TSC(g1, pb, ccol[:, 2:3], None, ALU.is_ge))
                E.waitc("dve", "dve")
                dve(lambda: STT(q, g1, -1e30, pb, ALU.mult, ALU.add))
                E.waitc("dve", "dve")
                dve(lambda: ncv.reduce_max(ccol[:, 3:4], q, axis=AX))
                E.waitc("dve", "dve")
                dve(lambda: TSC(g1, pb, ccol[:, 3:4], None, ALU.is_ge))
                E.waitc("dve", "dve")
                dve(lambda: TT(ccol[:, 4:5], ccol[:, 2:3], ccol[:, 3:4],
                               ALU.add))
                E.waitc("dve", "dve")
                dve(lambda: TSC(ccol[:, 4:5], ccol[:, 4:5], 1e-20, None,
                                ALU.max))
                E.waitc("dve", "dve")
                E.waitc("act", "dve")
                act(ccol[:, 5:6], ccol[:, 4:5], AF.Ln, milestone=False)
                act(ccol[:, 5:6], ccol[:, 5:6], AF.Exp, scale=-1.0)
                E.waitc("dve", "act")
                dve(lambda: TT(q, pb, g1, ALU.mult))
                E.waitc("dve", "dve")
                dve(lambda: TSC(cwtk[:, pcol], q, ccol[:, 5:6], 0.5,
                                ALU.mult, ALU.mult))
                E.waitc("dve", "dve")
            if stage == "router_tok":
                E.waitc("sp", "dve")
                E.waitc("sp", "act")
                dma("sx", outT_d[0:128, 0:40], sctok[:, 0:40])
                dma("sx", outT_d[0:128, 44:76], cwtk[:, :])
                fin = E.cnt["sx"]
                for eng in ENGS:
                    E.wait(eng, "sx", fin)
                raise _Stop
            E.waitc("pe", "dve")
            E.wait("pe", "act", led56[0]["act"])
            for sub in range(4):
                E.ins("pe", lambda s_=sub: nct.transpose(
                    ps[6][0:8, s_ * 128:(s_ + 1) * 128],
                    cwtk[:, s_ * 8:(s_ + 1) * 8], idtb[:, :]), [("pe", 1)])
            E.waitc("act", "pe")
            act(cwfm[0:8, t * TS:(t + 1) * TS], ps[6][0:8, 0:TS], AF.Copy)
            led56[0] = dict(E.cnt)
            led_cw[0] = dict(E.cnt)

        def ffn_pass(buf):
            if stage.startswith("ffn1eK"):
                n_e, n_t = int(stage[6:]), 1
            else:
                n_e = 1 if stage in ("ffn1g", "ffn1e1", "ffn1u", "ffn1b") else NE
                n_t = 1 if stage in ("ffn1g", "ffn1e1", "ffn1u", "ffn1b") else STW
            led_wgwu = dict(E.cnt)
            E.wait("pe", "dve", led56[0]["dve"])
            for e in range(n_e):
                if e == 0:
                    E.waitc("pool", "pe")
                    dma("wgq", wgb[:], wg_d[0])
                    dma("wgq", wub[:], wu_d[0])
                    led_wgwu = dict(E.cnt)
                    E.wait("pool", "pe", wd_free[0]["pe"])
                    dma("wdq0", wdb[0][:], wd_d[0, 0])
                for t in range(n_t):
                    if e < 8:
                        E.wait("pe", "act", led_cw[0]["act"])
                        E.wait("pe", "dve", ps4_free[0]["dve"])
                        mm(oneh8b[:, e * 128:(e + 1) * 128],
                           cwfm[0:8, t * TS:(t + 1) * TS], ps[4][:, :],
                           True, True, milestone=True)
                    E.wait("pe", "wgq", led_wgwu["wgq"])
                    for i in range(NI):
                        pg = ps[i % 4]
                        E.wait("pe", "act", psf[i % 4]["act"])
                        E.wait("pe", "dve", psf[i % 4]["dve"])
                        for k in range(NKD):
                            xnch = xn[:, k * STW * TS + t * TS:
                                      k * STW * TS + (t + 1) * TS]
                            mm(wgb[:, k * FI + i * 128:k * FI + (i + 1) * 128],
                               xnch, pg[:, :], k == 0, k == NKD - 1,
                               milestone=(k == NKD - 1))
                        E.waitc("act", "pe")
                        if i == 0:
                            E.wait("act", "dve", led_gu_free[0]["dve"])
                        act(gub[t][:, i * TS:(i + 1) * TS], pg[:, :], AF.Silu)
                        psf[i % 4] = dict(E.cnt)
                        if i % 2 == 1:
                            pump(1)
                    for i in range(NI):
                        pu = ps[i % 4]
                        E.wait("pe", "act", psf[i % 4]["act"])
                        E.wait("pe", "dve", psf[i % 4]["dve"])
                        for k in range(NKD):
                            xnch = xn[:, k * STW * TS + t * TS:
                                      k * STW * TS + (t + 1) * TS]
                            mm(wub[:, k * FI + i * 128:k * FI + (i + 1) * 128],
                               xnch, pu[:, :], k == 0, k == NKD - 1,
                               milestone=(k == NKD - 1))
                        E.waitc("dve", "pe")
                        E.waitc("dve", "act")
                        gu_sl = gub[t][:, i * TS:(i + 1) * TS]
                        dve(lambda g=gu_sl, pu=pu: TT(sq_dv[:], g, pu[:, :],
                                                      ALU.mult),
                            milestone=False)
                        if e < 8:
                            dve(lambda g=gu_sl: TT(g, sq_dv[:], ps[4][:, :],
                                                   ALU.mult))
                        else:
                            dve(lambda g=gu_sl: TSC(g, sq_dv[:], 0.5, None,
                                                    ALU.mult))
                        psf[i % 4] = dict(E.cnt)
                        if i % 2 == 1:
                            pump(1)
                    ps4_free[0] = dict(E.cnt)
                    if t == n_t - 1 and e + 1 < n_e:
                        E.waitc("pool", "pe")
                        dma("wgq", wgb[:], wg_d[e + 1])
                        dma("wgq", wub[:], wu_d[e + 1])
                        led_wgwu = dict(E.cnt)
                gu_ready = dict(E.cnt)
                for h in range(8):
                    nxt = (e, h + 1) if h + 1 < 8 else (e + 1, 0)
                    if nxt[0] < n_e:
                        E.wait("pool", "pe", wd_free[(h + 1) % 2]["pe"])
                        dma(f"wdq{(h + 1) % 2}", wdb[(h + 1) % 2][:],
                            wd_d[nxt[0], nxt[1]])
                    E.waitc("pe", f"wdq{h % 2}")
                    for t in range(n_t):
                        idx = h * STW + t
                        bb = (2, 3, 7)[idx % 3]
                        po = ps[bb]
                        E.wait("pe", "act", psf[bb]["act"])
                        E.wait("pe", "dve", psf[bb]["dve"])
                        if h == 0:
                            E.wait("pe", "dve", gu_ready["dve"])
                        for i in range(NI):
                            mm(wdb[h % 2][:, i * 128:(i + 1) * 128],
                               gub[t][:, i * TS:(i + 1) * TS], po[:, :],
                               i == 0, i == NI - 1, milestone=(i == NI - 1))
                        if t == n_t - 1:
                            wd_free[h % 2] = dict(E.cnt)
                        E.waitc("dve", "pe")
                        xch = xacc[buf][:, h * STW * TS + t * TS:
                                        h * STW * TS + (t + 1) * TS]
                        dve(lambda xch=xch, po=po: TT(xch, xch, po[:, :],
                                                      ALU.add))
                        psf[bb] = dict(E.cnt)
                        pump(2)
                led_gu_free[0] = dict(E.cnt)
            if stage == "ffn1e1" or stage.startswith("ffn1eK"):
                dump_f32(xacc[0], STW * TS)
                raise _Stop

        def front_pass(st, buf):
            ln_pass(buf, xw_dve=led_gu_free[0]["dve"])
            own = st >= 2
            nch_out = 32 if own else 16
            led6f = [dict(led56[0]), dict(led56[0])]
            E.wait("pe", "dve", led56[0]["dve"])
            for t in range(STW):
                gtile = st * STW + t
                for c in range(nch_out):
                    s_ip = c % 2
                    if c == 0:
                        E.wait("pool", "pe", ip_free[0]["pe"])
                        dma("ipq0", ipwb[0][:], ip_d[0])
                    if c + 1 < nch_out:
                        E.wait("pool", "pe", ip_free[(c + 1) % 2]["pe"])
                        dma(f"ipq{(c + 1) % 2}", ipwb[(c + 1) % 2][:],
                            ip_d[c + 1])
                    E.waitc("pe", f"ipq{s_ip}")
                    pz = ps[c % 4]
                    E.wait("pe", "act", psf[c % 4]["act"])
                    E.wait("pe", "dve", psf[c % 4]["dve"])
                    for k in range(NKD):
                        xnch = xn[:, k * STW * TS + t * TS:
                                  k * STW * TS + (t + 1) * TS]
                        mm(ipwb[s_ip][:, k * 128:(k + 1) * 128], xnch,
                           pz[:, :], k == 0, k == NKD - 1,
                           milestone=(k == NKD - 1))
                    ip_free[s_ip] = dict(E.cnt)
                    if c < 16:
                        if c == 0:
                            E.waitc("dve", "xdq")
                        E.waitc("dve", "act")
                        dve(lambda c=c: ncv.tensor_copy(
                            upw[:, 0:3], ccarry[:, c * 3:(c + 1) * 3]))
                        E.waitc("dve", "dve")
                        E.waitc("act", "dve")
                        E.waitc("act", "pe")
                        act(upw[:, 3:3 + TS], pz[:, :], AF.Copy)
                        psf[c % 4] = dict(E.cnt)
                        E.waitc("dve", "act")
                        dve(lambda c=c: TSC(sq_dv[:], upw[:, 0:TS],
                                            cw4b[:, c * 4:c * 4 + 1], None,
                                            ALU.mult), milestone=False)
                        for j in range(1, 4):
                            dve(lambda c=c, j=j: STT(
                                sq_dv[:], upw[:, j:j + TS],
                                cw4b[:, c * 4 + j:c * 4 + j + 1], sq_dv[:],
                                ALU.mult, ALU.add), milestone=False)
                        dve(lambda c=c: ncv.tensor_copy(
                            ccarry[:, c * 3:(c + 1) * 3], upw[:, TS:TS + 3]))
                        E.waitc("act", "dve")
                        E.waitc("act", "spq")
                        act(ut[:], sq_dv[:], AF.Silu)
                        bk = ps[6 - s_ip]
                        E.waitc("pe", "act")
                        E.wait("pe", "dve", led6f[s_ip]["dve"])
                        E.wait("pe", "act", led6f[s_ip]["act"])
                        mm(xpwb[:, c * 17:(c + 1) * 17], ut[:],
                           bk[0:17, :], True, True, milestone=True)
                        E.waitc("dve", "pe")
                        if c == 0:
                            dve(lambda bk=bk: ncv.tensor_copy(
                                xdbl[:], bk[0:17, :]))
                        else:
                            dve(lambda bk=bk: TT(xdbl[:], xdbl[:],
                                                 bk[0:17, :], ALU.add))
                        led6f[s_ip] = dict(E.cnt)
                        if own:
                            E.waitc("sp", "act")
                            dma("spq", uspill[c * 128:(c + 1) * 128,
                                              (gtile - 4) * TS:
                                              (gtile - 3) * TS], ut[:])
                        if c % 2 == 1:
                            pump(1)
                    else:
                        E.waitc("act", "pe")
                        E.waitc("act", "spq")
                        act(szt[:], pz[:, :], AF.Silu)
                        psf[c % 4] = dict(E.cnt)
                        if c % 2 == 1:
                            pump(1)
                        E.waitc("sp", "act")
                        dma("spq", szspill[(c - 16) * 128:(c - 15) * 128,
                                           (gtile - 4) * TS:(gtile - 3) * TS],
                            szt[:])
                E.waitc("pool", "dve")
                dma("xdq", bspill[:, gtile * TS:(gtile + 1) * TS], xdbl[:])
                front_done[gtile] = dict(E.cnt)
            led56[0] = dict(E.cnt)

        pend = []
        cur_gen = [None]
        brep_rd = [dict(E.cnt)]
        dtb_state = {"rd": dict(E.cnt)}
        dw_rd = [dict(E.cnt) for _ in range(DG)]
        dwadd = [dict(E.cnt) for _ in range(DG)]

        def pump(k=1):
            for _ in range(k):
                if cur_gen[0] is None:
                    if not pend:
                        return
                    cur_gen[0] = scan_units(pend.pop(0))
                try:
                    next(cur_gen[0])
                except StopIteration:
                    cur_gen[0] = None

        def pump_until(tw):
            while tw not in scan_done:
                if cur_gen[0] is None and not pend:
                    raise RuntimeError(f"scan {tw} never scheduled")
                pump(1)

        def scan_units(tw):
            own = tw >= 4
            E.waitc("pool", "xdq")
            dcols = slice(tw * TS, (tw + 1) * TS)
            slot_rd = [dict(E.cnt), dict(E.cnt)]

            E.wait("pool", "dve", dtb_state["rd"]["dve"])
            dma("dtq", sbr[:],
                bspill[0:1, dcols].partition_broadcast(128).squeeze(1))
            E.wait("pool", "dve", brep_rd[0]["dve"])
            for n in range(NSTATE):
                dma("bq0", brep[:, n * TS:(n + 1) * TS],
                    bspill[n + 1:n + 2, dcols]
                    .partition_broadcast(128).squeeze(1))
            bq_full = E.cnt["bq0"]
            for g in range(NG):
                for ci in range(DG):
                    c = g * DG + ci
                    dsl = dw[:, ci * TS:(ci + 1) * TS]
                    E.wait("dve", "act", dw_rd[ci]["act"])
                    E.waitc("dve", "dtq")
                    E.wait("dve", "pool", dw_rd[ci]["pool"])
                    dve(lambda d=dsl, c=c: TSC(d, sbr[:],
                                               dtwcolb[:, c:c + 1], None,
                                               ALU.mult))
                    dtb_state["rd"] = dict(E.cnt)
                    E.waitc("act", "dve")
                    act(sq_sp[:], dsl, AF.Abs, bias=dtbb[:, c:c + 1],
                        milestone=False)
                    act(sq_sp[:], sq_sp[:], AF.Exp, scale=-1.0,
                        milestone=False)
                    act(sq_sp[:], sq_sp[:], AF.Ln, bias=1.0, milestone=False)
                    act(dsl, dsl, AF.Relu, bias=dtbb[:, c:c + 1])
                    E.waitc("dve", "act")
                    dve(lambda d=dsl: TT(d, d, sq_sp[:], ALU.add))
                    dwadd[ci] = dict(E.cnt)
                    yield
                if own:
                    E.waitc("dve", "syq")
                    for ci in range(DG):
                        dve(lambda ci=ci: ncv.memset(
                            sy[:, ci * TS:(ci + 1) * TS], 0.0),
                            milestone=(ci == DG - 1))
                for n in range(NSTATE):
                    bsl = brep[:, n * TS:(n + 1) * TS]
                    for ci in range(DG):
                        c = g * DG + ci
                        slot = (n * DG + ci) % 2
                        dsl = dw[:, ci * TS:(ci + 1) * TS]
                        E.wait("act", "dve", max(slot_rd[slot]["dve"],
                                                 dwadd[ci]["dve"]))
                        act(sa[slot][:], dsl, AF.Exp,
                            scale=acolb[:, c * NSTATE + n:
                                        c * NSTATE + n + 1])
                        exp_led = dict(E.cnt)
                        if g == 0:
                            E.wait("pool", "bq0", bq_full)
                        E.wait("pool", "dve", max(slot_rd[slot]["dve"],
                                                  dwadd[ci]["dve"]))
                        pool(lambda s_=slot, d=dsl, b=bsl: ncp.tensor_tensor(
                            sdb[s_][:], d, b, ALU.mult))
                        db_led = dict(E.cnt)
                        dw_rd[ci] = dict(E.cnt)
                        E.wait("dve", "act", exp_led["act"])
                        E.wait("dve", "pool", db_led["pool"])
                        cc = c * NSTATE + n
                        dve(lambda s_=slot, cc=cc: ncv.tensor_tensor_scan(
                            sh[s_][:], sa[s_][:], sdb[s_][:],
                            carry[:, cc:cc + 1], ALU.mult, ALU.add))
                        E.waitc("dve", "dve")
                        dve(lambda s_=slot, cc=cc: ncv.tensor_copy(
                            carry[:, cc:cc + 1], sh[s_][:, TS - 1:TS]),
                            milestone=False)
                        if own:
                            dve(lambda s_=slot, b=bsl: TT(sdb[s_][:],
                                                          sh[s_][:], b,
                                                          ALU.mult))
                            E.waitc("pool", "dve")
                            pool(lambda ci=ci, s_=slot: ncp.tensor_tensor(
                                sy[:, ci * TS:(ci + 1) * TS],
                                sy[:, ci * TS:(ci + 1) * TS], sdb[s_][:],
                                ALU.add))
                        slot_rd[slot] = dict(E.cnt)
                        yield
                if own:
                    E.waitc("sp", "pool")
                    for ci in range(DG):
                        c = g * DG + ci
                        dma("syq", yspill[c * 128:(c + 1) * 128,
                                          (tw - 4) * TS:(tw - 3) * TS],
                            sy[:, ci * TS:(ci + 1) * TS])
                    yield
            brep_rd[0] = dict(E.cnt)
            scan_done[tw] = dict(E.cnt)

        def p4_gating(tile, buf):
            gt = tile - 4
            pump_until(tile)
            E.waitc("sp", "syq")
            E.waitc("sp", "spq")
            E.waitc("sp", "act")
            E.waitc("sp", "pe")
            rl2 = [(rld_y, rld_u, rld_sz), (sq_ln, ut, szt)]
            for c in range(NCH):
                pump(1)
                ry, ru, rsz = rl2[c % 2]
                E.wait("sp", "dve", rld_free[c % 2]["dve"])
                dma("rlq", ry[:], yspill[c * 128:(c + 1) * 128,
                                         gt * TS:(gt + 1) * TS])
                dma("rlq", ru[:], uspill[c * 128:(c + 1) * 128,
                                         gt * TS:(gt + 1) * TS])
                dma("rlq", rsz[:], szspill[c * 128:(c + 1) * 128,
                                           gt * TS:(gt + 1) * TS])
                E.waitc("dve", "rlq")
                dve(lambda c=c, ru=ru, ry=ry: STT(
                    sq_dv[:], ru[:], dcolb[:, c:c + 1], ry[:],
                    ALU.mult, ALU.add), milestone=False)
                ysl = ygt[:, c * TS:(c + 1) * TS]
                dve(lambda ysl=ysl, rsz=rsz: TT(ysl, sq_dv[:], rsz[:],
                                                ALU.mult))
                rld_free[c % 2] = dict(E.cnt)
            E.waitc("pe", "dve")
            E.wait("pool", "pe", ow_free[0]["pe"])
            dma("owq0", owb[0][:], ow_d[0])
            for h in range(8):
                if h + 1 < 8:
                    E.wait("pool", "pe", ow_free[(h + 1) % 2]["pe"])
                    dma(f"owq{(h + 1) % 2}", owb[(h + 1) % 2][:],
                        ow_d[h + 1])
                E.waitc("pe", f"owq{h % 2}")
                bb = (2, 3, 7)[h % 3]
                po = ps[bb]
                E.wait("pe", "act", psf[bb]["act"])
                E.wait("pe", "dve", psf[bb]["dve"])
                for c in range(NCH):
                    mm(owb[h % 2][:, c * 128:(c + 1) * 128],
                       ygt[:, c * TS:(c + 1) * TS], po[:, :],
                       c == 0, c == NCH - 1, milestone=(c == NCH - 1))
                ow_free[h % 2] = dict(E.cnt)
                E.waitc("dve", "pe")
                xch = xacc[buf][:, h * STW * TS + (tile % STW) * TS:
                                h * STW * TS + (tile % STW + 1) * TS]
                dve(lambda xch=xch, po=po, h=h: STT(
                    xch, po[:, :], obb[:, h:h + 1], xch, ALU.add, ALU.add))
                psf[bb] = dict(E.cnt)

        def dump_f32(buf, ncols):
            E.waitc("sp", "dve")
            E.waitc("sp", "act")
            E.waitc("sp", "pe")
            E.waitc("sp", "pool")
            for ch in range(NKD):
                dma("sx", outT_d[ch * 128:(ch + 1) * 128, 0:ncols],
                    buf[:, ch * ncols:(ch + 1) * ncols])
            fin = E.cnt["sx"]
            for eng in ENGS:
                E.wait(eng, "sx", fin)

        def dump_tile(ap128, col0=0):
            E.waitc("sp", "dve")
            E.waitc("sp", "act")
            E.waitc("sp", "pe")
            E.waitc("sp", "pool")
            dma("sx", outT_d[0:128, col0:col0 + TS], ap128)
            fin = E.cnt["sx"]
            for eng in ENGS:
                E.wait(eng, "sx", fin)

        for st in range(NSTS):
            buf = 0
            if st >= 1:
                fd = front_done[(st - 1) * STW + STW - 1]
                E.wait("sp", "dve", fd["dve"])
                E.wait("sp", "pe", fd["pe"])
                if st >= 3:
                    E.wait("sp", "sx", xspill_done[st - 1]["sx"])
            for ch in range(NKD):
                dma("sx", xacc[buf][:, ch * STW * TS:(ch + 1) * STW * TS],
                    xT_d[ch * 128:(ch + 1) * 128,
                         st * STW * TS:(st + 1) * STW * TS])
            xl = E.cnt["sx"]
            for eng in ("pe", "act", "dve"):
                E.wait(eng, "sx", xl)
            if stage == "skeleton":
                dump_f32(xacc[0], STW * TS)
                return
            ln_pass(buf, router=True)
            if stage == "ln":
                E.waitc("dve", "dve")
                E.waitc("pool", "dve")
                for ch in range(NKD):
                    dma("xdq", outT_d[ch * 128:(ch + 1) * 128, 0:STW * TS],
                        xn[:, ch * STW * TS:(ch + 1) * STW * TS])
                fin = E.cnt["xdq"]
                for eng in ENGS:
                    E.wait(eng, "xdq", fin)
                return
            if stage == "router":
                E.waitc("pool", "act")
                dma("xdq", outT_d[0:8, 0:STW * TS], cwfm[:, :])
                fin = E.cnt["xdq"]
                for eng in ENGS:
                    E.wait(eng, "xdq", fin)
                return
            ffn_pass(buf)
            if stage == "ffn1":
                dump_f32(xacc[0], STW * TS)
                return
            front_pass(st, buf)
            if stage == "front":
                dump_tile(xdbl[0:17, :], 0)
                return
            if st >= 2:
                E.waitc("sp", "dve")
                E.waitc("sp", "pe")
                for ch in range(NKD):
                    dma("sx", xaspill[ch * 128:(ch + 1) * 128,
                                       (st - 2) * STW * TS:
                                       (st - 1) * STW * TS],
                        xacc[buf][:, ch * STW * TS:(ch + 1) * STW * TS])
                xspill_done[st] = dict(E.cnt)
            pend.extend([st * STW, st * STW + 1])

        for ost in range(2):
            st = 2 + ost
            buf = 0
            E.wait("sp", "sx", xspill_done[st]["sx"])
            E.waitc("sp", "dve")
            E.waitc("sp", "pe")
            for ch in range(NKD):
                dma("sx", xacc[buf][:, ch * STW * TS:(ch + 1) * STW * TS],
                    xaspill[ch * 128:(ch + 1) * 128,
                            ost * STW * TS:(ost + 1) * STW * TS])
            rl = E.cnt["sx"]
            for eng in ("pe", "act", "dve"):
                E.wait(eng, "sx", rl)
            for t in range(STW):
                p4_gating(st * STW + t, buf)
            ln_pass(buf, router=True, xw_dve=E.cnt["dve"])
            ffn_pass(buf)
            E.waitc("sp", "dve")
            for ch in range(NKD):
                dma("sx", outT_d[ch * 128:(ch + 1) * 128,
                                  ost * STW * TS:(ost + 1) * STW * TS],
                    xacc[buf][:, ch * STW * TS:(ch + 1) * STW * TS])

        for eng in ENGS:
            for sm in ("sx", "spq", "rlq", "wgq", "wdq0", "wdq1", "ipq0",
                       "ipq1", "owq0", "owq1", "xdq", "syq", "bq0",
                       "dtq"):
                E.waitc(eng, sm)

    handles = dict(pe=nc.tensor, act=nc.scalar, dve=nc.vector,
                   pool=nc.gpsimd, sp=nc.sync)
    E = Emitter(nc, sems, handles)
    finals = {}

    def _run(name):
        E.begin(name)
        try:
            emit_all(E)
        except _Stop:
            pass
        finals[name] = dict(E.cnt)

    @block.sync
    def _(_s):
        _run("sp")

    @block.gpsimd
    def _(_g):
        _run("pool")

    @block.scalar
    def _(_s):
        _run("act")

    @block.vector
    def _(_v):
        _run("dve")

    @block.tensor
    def _(_t):
        _run("pe")

    stack.close()
    vals = list(finals.values())
    for v in vals[1:]:
        assert v == vals[0], f"ledger mismatch: {vals[0]} vs {v}"
    return nc, rem


def _host_stage(inp):
    f = np.ascontiguousarray
    x = np.asarray(inp["L0_attn_x"], dtype=np.float32)
    wg = np.asarray(inp["L0_ffn_wg"], np.float32)
    wu = np.asarray(inp["L0_ffn_wu"], np.float32)
    wd = np.asarray(inp["L0_ffn_wd"], np.float32)
    wgx = np.concatenate([wg, np.asarray(inp["L0_ffn_wg_s"],
                                         np.float32)[None]], 0)
    wux = np.concatenate([wu, np.asarray(inp["L0_ffn_wu_s"],
                                         np.float32)[None]], 0)
    wdx = np.concatenate([wd, np.asarray(inp["L0_ffn_wd_s"],
                                         np.float32)[None]], 0)
    wg_sw = f(wgx.reshape(NE, NKD, 128, FI).transpose(0, 2, 1, 3)
              .reshape(NE, 128, NKD * FI))
    wu_sw = f(wux.reshape(NE, NKD, 128, FI).transpose(0, 2, 1, 3)
              .reshape(NE, 128, NKD * FI))
    wd_sw = f(wdx.reshape(NE, NI, 128, 8, 128).transpose(0, 3, 2, 1, 4)
              .reshape(NE, 8, 128, NI * 128))
    ipT = f(np.asarray(inp["L0_attn_in_proj_w"], np.float32).T)
    ip_sw = f(ipT.reshape(NKD, 128, 32, 128).transpose(2, 1, 0, 3)
              .reshape(32, 128, NKD * 128))
    owT = f(np.asarray(inp["L0_attn_out_w"], np.float32).T)
    ow_sw = f(owT.reshape(NCH, 128, 8, 128).transpose(2, 1, 0, 3)
              .reshape(8, 128, NCH * 128))
    xpT = f(np.asarray(inp["L0_attn_x_proj_w"], np.float32).T)
    xp_sw = f(xpT.reshape(NCH, 128, 17).transpose(1, 0, 2).reshape(128, -1))
    rwT = f(np.asarray(inp["L0_ffn_w_router"], np.float32).T)
    rw_sw = f(rwT.reshape(NKD, 128, 8).transpose(1, 0, 2).reshape(128, -1))
    cw = np.asarray(inp["L0_attn_conv_w"], np.float32)[:, 0, :]
    cw4_sw = f(cw.reshape(NCH, 128, 4).transpose(1, 0, 2).reshape(128, -1))
    dtw_sw = f(np.asarray(inp["L0_attn_dt_w"], np.float32)
               .reshape(NCH, 128).T)
    dtb_sw = f(np.asarray(inp["L0_attn_dt_b"], np.float32)
               .reshape(NCH, 128).T)
    a_sw = f(np.asarray(inp["L0_attn_A_param"], np.float32)
             .reshape(NCH, 128, NSTATE).transpose(1, 0, 2).reshape(128, -1))
    d_sw = f(np.asarray(inp["L0_attn_D_param"], np.float32)
             .reshape(NCH, 128).T)
    ob_sw = f(np.asarray(inp["L0_attn_out_b"], np.float32).reshape(8, 128).T)
    ones_col = np.ones((128, 1), np.float32)
    ones_row = np.ones((1, 128), np.float32)
    id128 = np.eye(128, dtype=np.float32)
    oneh8 = np.zeros((8, 8 * 128), np.float32)
    for e in range(8):
        oneh8[e, e * 128:(e + 1) * 128] = 1.0

    shared = dict(wg_sw=wg_sw, wu_sw=wu_sw, wd_sw=wd_sw, ip_sw=ip_sw,
                  ow_sw=ow_sw, xp_sw=xp_sw, rw_sw=rw_sw, cw4_sw=cw4_sw,
                  dtw_sw=dtw_sw, dtb_sw=dtb_sw, a_sw=a_sw, d_sw=d_sw,
                  ob_sw=ob_sw, ones_col=ones_col, ones_row=ones_row,
                  id128=id128, oneh8=oneh8)
    in_maps = []
    for c in range(NCORES):
        b, role = c >> 1, c & 1
        half0 = x[b, 0:2048]
        if role == 0:
            virt = np.concatenate([np.zeros_like(half0), half0], 0)
        else:
            virt = np.concatenate([half0, x[b, 2048:4096]], 0)
        m = dict(shared)
        m["xT"] = f(virt.T)
        in_maps.append(m)
    return in_maps


def kernel(**inputs):
    in_maps = _host_stage(inputs)
    nc, _rem = build()
    res = run_bass_kernel_spmd(nc, in_maps, core_ids=list(range(NCORES)))
    out = np.empty((4, 4096, 1024), np.float32)
    for c in range(NCORES):
        b, role = c >> 1, c & 1
        o = res.results[c]["outT"]
        out[b, role * 2048:(role + 1) * 2048] = o.T
    return out


# version 15 (final)
# speedup vs baseline: 1.1592x; 1.1592x over previous
import numpy as np
import ml_dtypes
import concourse.bass as bass
import concourse.mybir as mybir
from concourse.bass_utils import run_bass_kernel_spmd

F32 = mybir.dt.float32
BF16 = mybir.dt.bfloat16
AF = mybir.ActivationFunctionType
ALU = mybir.AluOpType
AX = mybir.AxisListType.X

NCORES = 8
L = 4096
TS = 512
NT = L // TS
STW = 2
NSTS = NT // STW
D = 1024
DI = 2048
NSTATE = 16
FI = 1408
NI = 11
NKD = 8
NCH = 16
NE = 9
DG = 2
NG = NCH // DG
EPS = 1e-5

ENGS = ("pe", "act", "dve", "pool", "sp")


class _Stop(Exception):
    pass


class Emitter:
    def __init__(self, nc, sems, handles):
        self.nc = nc
        self.sems = sems
        self.h = handles
        self.cur = None
        self.cnt = {k: 0 for k in sems}

    def begin(self, cur):
        self.cur = cur
        self.cnt = {k: 0 for k in self.sems}

    def ins(self, eng, fn, incs=()):
        inst = fn() if self.cur == eng else None
        for s, n in incs:
            self.cnt[s] += n
            if inst is not None:
                inst.then_inc(self.sems[s], n)

    def wait(self, eng, sem, val):
        if self.cur == eng and val > 0:
            self.h[eng].wait_ge(self.sems[sem], val)

    def waitc(self, eng, sem):
        self.wait(eng, sem, self.cnt[sem])


def build(stage="full", detect_races=True):
    nc = bass.Bass("TRN2", target_bir_lowering=False, debug=False,
                   detect_race_conditions=detect_races)

    def din(name, shape, dt=F32):
        return nc.dram_tensor(name, list(shape), dt, kind="ExternalInput").ap()

    xT_d = din("xT", [D, L])
    wg_d = din("wg_sw", [NE, 128, NKD * FI])
    wu_d = din("wu_sw", [NE, 128, NKD * FI])
    wd_d = din("wd_sw", [NE, 8, 128, NI * 128])
    ip_d = din("ip_sw", [32, 128, NKD * 128])
    ow_d = din("ow_sw", [8, 128, NCH * 128])
    xp_d = din("xp_sw", [128, NCH * 17])
    rw_d = din("rw_sw", [128, NKD * 8])
    cw4_d = din("cw4_sw", [128, NCH * 4])
    dtwc_d = din("dtw_sw", [128, NCH])
    dtb_d = din("dtb_sw", [128, NCH])
    ap_d = din("a_sw", [128, NCH * NSTATE])
    dcol_d = din("d_sw", [128, NCH])
    ob_d = din("ob_sw", [128, 8])
    ones_c = din("ones_col", [128, 1])
    ones_r = din("ones_row", [1, 128])
    idt_d = din("id128", [128, 128])
    oneh8_d = din("oneh8", [8, 8 * 128])

    outT_d = nc.dram_tensor("outT", [D, L // 2], F32, kind="ExternalOutput").ap()

    uspill = nc.dram_tensor("uspill", [DI, L // 2], F32, kind="Internal").ap()
    szspill = nc.dram_tensor("szspill", [DI, L // 2], F32, kind="Internal").ap()
    yspill = nc.dram_tensor("yspill", [DI, L // 2], F32, kind="Internal").ap()
    xaspill = nc.dram_tensor("xaspill", [D, L // 2], F32, kind="Internal").ap()
    bspill = nc.dram_tensor("bspill", [17, L], F32, kind="Internal").ap()

    A = nc.alloc_sbuf_tensor
    xacc = [A("xacc0", [128, NKD * STW * TS], F32)]
    xn = A("xn_t", [128, NKD * STW * TS], BF16)
    cwfm = A("cwfm", [8, STW * TS], BF16)
    wgb = A("wgb", [128, NKD * FI], BF16)
    wub = A("wub", [128, NKD * FI], BF16)
    wdb = [A(f"wdb{i}", [128, NI * 128], BF16) for i in range(2)]
    bigb = A("bigb", [128, 2 * NI * TS], BF16)
    gub = [bigb[:, i * NI * TS:(i + 1) * NI * TS] for i in range(2)]
    ipwb = [A(f"ipwb{i}", [128, NKD * 128], BF16) for i in range(2)]
    owb = [A(f"owb{i}", [128, NCH * 128], BF16) for i in range(2)]
    upw = A("upw", [128, TS + 3], F32)
    ccarry = A("ccarry", [128, NCH * 3], F32)
    ut = A("ut", [128, TS], F32)
    szt = A("szt", [128, TS], F32)
    rld_u = A("rld_u", [128, TS], F32)
    rld_sz = A("rld_sz", [128, TS], F32)
    rld_y = A("rld_y", [128, TS], F32)
    ygt = bigb[:, 0:NCH * TS]
    dw = A("dw", [128, DG * TS], F32)
    brep = A("brep", [128, NSTATE * TS], BF16)
    sa = [A(f"sa{i}", [128, TS], F32) for i in range(2)]
    sdb = [A(f"sdb{i}", [128, TS], F32) for i in range(2)]
    sh = [A(f"sh{i}", [128, TS], F32) for i in range(2)]
    sbr = A("sbr", [128, TS], F32)
    sy = A("sy", [128, DG * TS], F32)
    carry = A("carry", [128, 256], F32)
    xdbl = A("xdbl", [17, TS], F32)
    sq_ln = A("sq_ln", [128, TS], F32)
    sq_sp = A("sq_sp", [128, TS], F32)
    sq_dv = A("sq_dv", [128, TS], F32)
    lnr = A("lnr", [1, 3 * TS], F32)
    scsb = A("scsb", [8, TS], F32)
    sctok = A("sctok", [128, 44], F32)
    cwork = A("cwork", [128, 4 * 8], F32)
    cwork2 = A("cwork2", [128, 4 * 8], F32)
    cwork3 = A("cwork3", [128, 4 * 8], F32)
    ccol = A("ccol", [128, 8], F32)
    cwtk = A("cwtk", [128, 4 * 8], F32)
    kc = A("kc", [1, 4], F32)
    rwf = A("rwf", [128, NKD * 8], F32)
    wsr = A("wsr", [128, 8], F32)
    xpwb = A("xpwb", [128, NCH * 17], F32)
    cw4b = A("cw4b", [128, NCH * 4], F32)
    dtwcolb = A("dtwcolb", [128, NCH], F32)
    dtbb = A("dtbb", [128, NCH], F32)
    acolb = A("acolb", [128, NCH * NSTATE], F32)
    dcolb = A("dcolb", [128, NCH], F32)
    obb = A("obb", [128, 8], F32)
    onecb = A("onecb", [128, 1], F32)
    onerb = A("onerb", [1, 128], F32)
    idtb = A("idtb", [128, 128], F32)
    oneh8b = A("oneh8b", [8, 8 * 128], BF16)

    rem = nc.sbuf_bytes_remaining

    P_ = nc.alloc_psum_tensor
    ps = [P_(f"ps{i}", [128, TS], F32) for i in range(8)]

    sem_names = ["sx", "spq", "rlq", "wgq", "wdq0", "wdq1", "ipq0", "ipq1",
                 "owq0", "owq1", "xdq", "syq", "bq0", "dtq",
                 "pe", "act", "dve", "pool"]

    import contextlib
    stack = contextlib.ExitStack()
    sems = {n: stack.enter_context(nc.semaphore(n)) for n in sem_names}
    block = stack.enter_context(nc.Block())

    def emit_all(E: Emitter, stage=stage):
        nct, nca, ncv, ncp, ncs = nc.tensor, nc.scalar, nc.vector, nc.gpsimd, nc.sync
        TT = ncv.tensor_tensor
        TSC = ncv.tensor_scalar
        STT = ncv.scalar_tensor_tensor

        SYNC_SEMS = ("sx", "spq", "rlq", "syq")

        def dma(q, out, in_):
            eng = "sp" if q in SYNC_SEMS else "pool"
            obj = ncs if q in SYNC_SEMS else ncp
            E.ins(eng, lambda: obj.dma_start(out=out, in_=in_), [(q, 16)])

        def mm(lhsT, rhs, out, start, stop, milestone=False):
            incs = [("pe", 1)] if milestone else ()
            E.ins("pe", lambda: nct.matmul(out, lhsT, rhs, start=start,
                                           stop=stop), incs)

        def act(out, in_, func, milestone=True, **kw):
            incs = [("act", 1)] if milestone else ()
            E.ins("act", lambda: nca.activation(out, in_, func, **kw), incs)

        def dve(fn, milestone=True):
            incs = [("dve", 1)] if milestone else ()
            E.ins("dve", fn, incs)

        def pool(fn, milestone=True):
            incs = [("pool", 1)] if milestone else ()
            E.ins("pool", fn, incs)

        dma("sx", xpwb[:], xp_d)
        dma("sx", cw4b[:], cw4_d)
        dma("sx", dtbb[:], dtb_d)
        dma("sx", acolb[:], ap_d)
        dma("sx", dcolb[:], dcol_d)
        dma("sx", obb[:], ob_d)
        dma("sx", onecb[:], ones_c)
        dma("sx", onerb[:], ones_r)
        dma("sx", idtb[:], idt_d)
        dma("sx", rwf[:], rw_d)
        dma("sx", dtwcolb[:], dtwc_d)
        dma("wgq", oneh8b[:], oneh8_d)
        E.waitc("sp", "sx")
        for eng in ("pe", "act", "dve", "pool"):
            E.waitc(eng, "sx")
            E.waitc(eng, "wgq")
        pool(lambda: ncp.memset(kc[0:1, 0:1], EPS))
        pool(lambda: ncp.memset(kc[0:1, 1:2], -0.5))
        E.waitc("act", "pool")
        E.waitc("dve", "pool")
        act(acolb[:], acolb[:], AF.Exp)
        E.waitc("dve", "act")
        dve(lambda: TSC(acolb[:], acolb[:], -1.0, None, ALU.mult))
        dve(lambda: ncv.memset(ccarry[:], 0.0))
        dve(lambda: ncv.memset(carry[:], 0.0))
        for k in range(NKD):
            mm(onecb[:], rwf[:, k * 8:(k + 1) * 8], ps[5][0:1, 0:8],
               k == 0, k == NKD - 1, milestone=(k == NKD - 1))
        E.waitc("act", "pe")
        act(lnr[0:1, 0:8], ps[5][0:1, 0:8], AF.Copy)
        E.waitc("pe", "act")
        mm(onerb[:], lnr[0:1, 0:8], ps[5][:, 0:8], True, True,
           milestone=True)
        E.waitc("act", "pe")
        act(wsr[:], ps[5][:, 0:8], AF.Copy)

        front_done = {}
        scan_done = {}
        xspill_done = {}
        led_gu_free = [dict(E.cnt)]
        led_cw = [dict(E.cnt)]
        psf = {b: dict(E.cnt) for b in (0, 1, 2, 3, 7)}
        ip_free = [dict(E.cnt), dict(E.cnt)]
        wd_free = [dict(E.cnt), dict(E.cnt)]
        ow_free = [dict(E.cnt), dict(E.cnt)]
        rld_free = [dict(E.cnt), dict(E.cnt)]
        ps4_free = [dict(E.cnt)]
        led56 = [dict(E.cnt)]
        led7 = [dict(E.cnt)]

        def ln_pass(buf, router=False, xw_dve=0):
            for t in range(STW if not stage.startswith("ln_") else 1):
                E.wait("pe", "dve", max(led56[0]["dve"], xw_dve))
                E.wait("pe", "act", led56[0]["act"])
                for ch in range(NKD):
                    xch = xacc[buf][:, ch * STW * TS + t * TS:
                                    ch * STW * TS + (t + 1) * TS]
                    mm(onecb[:], xch, ps[5][0:1, :], ch == 0, ch == NKD - 1,
                       milestone=(ch == NKD - 1))
                for ch in range(NKD):
                    xch = xacc[buf][:, ch * STW * TS + t * TS:
                                    ch * STW * TS + (t + 1) * TS]
                    E.waitc("act", "pe") if ch > 0 else None
                    if ch == 0:
                        E.waitc("act", "dve")
                    act(sq_ln[:], xch, AF.Square)
                    E.waitc("pe", "act")
                    mm(onecb[:], sq_ln[:], ps[6][0:1, :], ch == 0,
                       ch == NKD - 1, milestone=True)
                E.waitc("dve", "pe")
                r_mean = lnr[0:1, 0:TS]
                r_var = lnr[0:1, TS:2 * TS]
                r_mrs = lnr[0:1, 2 * TS:3 * TS]
                dve(lambda: TSC(r_mean, ps[5][0:1, :], 1.0 / D, None,
                                ALU.mult), milestone=False)
                dve(lambda: TSC(r_var, ps[6][0:1, :], 1.0 / D, None,
                                ALU.mult))
                E.waitc("dve", "dve")
                dve(lambda: TT(r_mrs, r_mean, r_mean, ALU.mult))
                E.waitc("dve", "dve")
                dve(lambda: TT(r_var, r_var, r_mrs, ALU.subtract))
                E.waitc("act", "dve")
                act(r_var, r_var, AF.Ln, bias=kc[0:1, 0:1], milestone=False)
                act(r_var, r_var, AF.Exp, scale=kc[0:1, 1:2])
                E.waitc("dve", "act")
                dve(lambda: TT(r_mrs, r_mean, r_var, ALU.mult))
                led56[0] = dict(E.cnt)
                E.wait("pe", "dve", led56[0]["dve"])
                mm(onerb[:], r_var, ps[5][:, :], True, True)
                mm(onerb[:], r_mrs, ps[6][:, :], True, True, milestone=True)
                E.waitc("dve", "pe")
                for ch in range(NKD):
                    xch = xacc[buf][:, ch * STW * TS + t * TS:
                                    ch * STW * TS + (t + 1) * TS]
                    xnch = xn[:, ch * STW * TS + t * TS:
                              ch * STW * TS + (t + 1) * TS]
                    dve(lambda xch=xch: TT(sq_dv[:], xch, ps[5][:, :],
                                           ALU.mult), milestone=False)
                    dve(lambda xnch=xnch: TT(xnch, sq_dv[:], ps[6][:, :],
                                             ALU.subtract),
                        milestone=(ch == NKD - 1))
                led56[0] = dict(E.cnt)
                if router:
                    router_tile(t, buf)

        def router_tile(t, buf):
            E.wait("pe", "dve", led56[0]["dve"])
            E.wait("pe", "act", led56[0]["act"])
            for k in range(NKD):
                xch = xacc[buf][:, k * STW * TS + t * TS:
                                k * STW * TS + (t + 1) * TS]
                mm(rwf[:, k * 8:(k + 1) * 8], xch, ps[5][0:8, :],
                   k == 0, k == NKD - 1, milestone=(k == NKD - 1))
            E.waitc("act", "pe")
            act(scsb[:], ps[5][0:8, :], AF.Copy)
            led56[0] = dict(E.cnt)
            E.waitc("pe", "act")
            r_mean = lnr[0:1, 0:TS]
            r_rstd = lnr[0:1, TS:2 * TS]
            for sub in range(4):
                E.ins("pe", lambda s_=sub: nct.transpose(
                    ps[6][:, s_ * 8:(s_ + 1) * 8],
                    scsb[0:8, s_ * 128:(s_ + 1) * 128], idtb[0:8, 0:8]),
                    [("pe", 1)])
                E.ins("pe", lambda s_=sub: nct.transpose(
                    ps[6][:, 32 + s_:33 + s_],
                    r_mean[0:1, s_ * 128:(s_ + 1) * 128], idtb[0:1, 0:1]),
                    [("pe", 1)])
                E.ins("pe", lambda s_=sub: nct.transpose(
                    ps[6][:, 36 + s_:37 + s_],
                    r_rstd[0:1, s_ * 128:(s_ + 1) * 128], idtb[0:1, 0:1]),
                    [("pe", 1)])
            E.waitc("act", "pe")
            act(sctok[:, 0:40], ps[6][:, 0:40], AF.Copy)
            led56[0] = dict(E.cnt)
            E.waitc("dve", "act")
            for sub in range(4):
                pcol = slice(sub * 8, (sub + 1) * 8)
                mcol = sctok[:, 32 + sub:33 + sub]
                rcol = sctok[:, 36 + sub:37 + sub]
                sto = sctok[:, pcol]
                q = cwork2[:, pcol]
                pb = cwork[:, pcol]
                g1 = cwork3[:, pcol]
                dve(lambda: TSC(pb, wsr[:, :], mcol, None, ALU.mult))
                E.waitc("dve", "dve")
                dve(lambda: TT(q, sto, pb, ALU.subtract))
                E.waitc("dve", "dve")
                dve(lambda: ncv.reduce_max(ccol[:, 0:1], q, axis=AX,
                                           negate=True))
                E.waitc("dve", "dve")
                dve(lambda: TT(ccol[:, 1:2], ccol[:, 0:1], rcol, ALU.mult))
                E.waitc("dve", "dve")
                E.waitc("act", "dve")
                act(pb, q, AF.Exp, scale=rcol, bias=ccol[:, 1:2])
                E.waitc("dve", "act")
                dve(lambda: ncv.reduce_max(ccol[:, 2:3], pb, axis=AX))
                E.waitc("dve", "dve")
                dve(lambda: TSC(g1, pb, ccol[:, 2:3], None, ALU.is_ge))
                E.waitc("dve", "dve")
                dve(lambda: STT(q, g1, -1e30, pb, ALU.mult, ALU.add))
                E.waitc("dve", "dve")
                dve(lambda: ncv.reduce_max(ccol[:, 3:4], q, axis=AX))
                E.waitc("dve", "dve")
                dve(lambda: TSC(g1, pb, ccol[:, 3:4], None, ALU.is_ge))
                E.waitc("dve", "dve")
                dve(lambda: TT(ccol[:, 4:5], ccol[:, 2:3], ccol[:, 3:4],
                               ALU.add))
                E.waitc("dve", "dve")
                dve(lambda: TSC(ccol[:, 4:5], ccol[:, 4:5], 1e-20, None,
                                ALU.max))
                E.waitc("dve", "dve")
                E.waitc("act", "dve")
                act(ccol[:, 5:6], ccol[:, 4:5], AF.Ln, milestone=False)
                act(ccol[:, 5:6], ccol[:, 5:6], AF.Exp, scale=-1.0)
                E.waitc("dve", "act")
                dve(lambda: TT(q, pb, g1, ALU.mult))
                E.waitc("dve", "dve")
                dve(lambda: TSC(cwtk[:, pcol], q, ccol[:, 5:6], 0.5,
                                ALU.mult, ALU.mult))
                E.waitc("dve", "dve")
            if stage == "router_tok":
                E.waitc("sp", "dve")
                E.waitc("sp", "act")
                dma("sx", outT_d[0:128, 0:40], sctok[:, 0:40])
                dma("sx", outT_d[0:128, 44:76], cwtk[:, :])
                fin = E.cnt["sx"]
                for eng in ENGS:
                    E.wait(eng, "sx", fin)
                raise _Stop
            E.waitc("pe", "dve")
            E.wait("pe", "act", led56[0]["act"])
            for sub in range(4):
                E.ins("pe", lambda s_=sub: nct.transpose(
                    ps[6][0:8, s_ * 128:(s_ + 1) * 128],
                    cwtk[:, s_ * 8:(s_ + 1) * 8], idtb[:, :]), [("pe", 1)])
            E.waitc("act", "pe")
            act(cwfm[0:8, t * TS:(t + 1) * TS], ps[6][0:8, 0:TS], AF.Copy)
            led56[0] = dict(E.cnt)
            led_cw[0] = dict(E.cnt)

        def ffn_pass(buf):
            if stage.startswith("ffn1eK"):
                n_e, n_t = int(stage[6:]), 1
            else:
                n_e = 1 if stage in ("ffn1g", "ffn1e1", "ffn1u", "ffn1b") else NE
                n_t = 1 if stage in ("ffn1g", "ffn1e1", "ffn1u", "ffn1b") else STW
            led_wgwu = dict(E.cnt)
            E.wait("pe", "dve", led56[0]["dve"])
            for e in range(n_e):
                if e == 0:
                    E.waitc("pool", "pe")
                    dma("wgq", wgb[:], wg_d[0])
                    dma("wgq", wub[:], wu_d[0])
                    led_wgwu = dict(E.cnt)
                    E.wait("pool", "pe", wd_free[0]["pe"])
                    dma("wdq0", wdb[0][:], wd_d[0, 0])
                for t in range(n_t):
                    if e < 8:
                        E.wait("pe", "act", led_cw[0]["act"])
                        E.wait("pe", "dve", ps4_free[0]["dve"])
                        mm(oneh8b[:, e * 128:(e + 1) * 128],
                           cwfm[0:8, t * TS:(t + 1) * TS], ps[4][:, :],
                           True, True, milestone=True)
                    E.wait("pe", "wgq", led_wgwu["wgq"])
                    for i in range(NI):
                        pg = ps[i % 4]
                        E.wait("pe", "act", psf[i % 4]["act"])
                        E.wait("pe", "dve", psf[i % 4]["dve"])
                        for k in range(NKD):
                            xnch = xn[:, k * STW * TS + t * TS:
                                      k * STW * TS + (t + 1) * TS]
                            mm(wgb[:, k * FI + i * 128:k * FI + (i + 1) * 128],
                               xnch, pg[:, :], k == 0, k == NKD - 1,
                               milestone=(k == NKD - 1))
                        E.waitc("act", "pe")
                        if i == 0:
                            E.wait("act", "dve", led_gu_free[0]["dve"])
                        act(gub[t][:, i * TS:(i + 1) * TS], pg[:, :], AF.Silu)
                        psf[i % 4] = dict(E.cnt)
                        if i % 2 == 1:
                            pump(1)
                    for i in range(NI):
                        pu = ps[i % 4]
                        E.wait("pe", "act", psf[i % 4]["act"])
                        E.wait("pe", "dve", psf[i % 4]["dve"])
                        for k in range(NKD):
                            xnch = xn[:, k * STW * TS + t * TS:
                                      k * STW * TS + (t + 1) * TS]
                            mm(wub[:, k * FI + i * 128:k * FI + (i + 1) * 128],
                               xnch, pu[:, :], k == 0, k == NKD - 1,
                               milestone=(k == NKD - 1))
                        E.waitc("dve", "pe")
                        E.waitc("dve", "act")
                        gu_sl = gub[t][:, i * TS:(i + 1) * TS]
                        dve(lambda g=gu_sl, pu=pu: TT(sq_dv[:], g, pu[:, :],
                                                      ALU.mult),
                            milestone=False)
                        if e < 8:
                            dve(lambda g=gu_sl: TT(g, sq_dv[:], ps[4][:, :],
                                                   ALU.mult))
                        else:
                            dve(lambda g=gu_sl: TSC(g, sq_dv[:], 0.5, None,
                                                    ALU.mult))
                        psf[i % 4] = dict(E.cnt)
                        if i % 2 == 1:
                            pump(1)
                    ps4_free[0] = dict(E.cnt)
                    if t == n_t - 1 and e + 1 < n_e:
                        E.waitc("pool", "pe")
                        dma("wgq", wgb[:], wg_d[e + 1])
                        dma("wgq", wub[:], wu_d[e + 1])
                        led_wgwu = dict(E.cnt)
                gu_ready = dict(E.cnt)
                for h in range(8):
                    nxt = (e, h + 1) if h + 1 < 8 else (e + 1, 0)
                    if nxt[0] < n_e:
                        E.wait("pool", "pe", wd_free[(h + 1) % 2]["pe"])
                        dma(f"wdq{(h + 1) % 2}", wdb[(h + 1) % 2][:],
                            wd_d[nxt[0], nxt[1]])
                    E.waitc("pe", f"wdq{h % 2}")
                    for t in range(n_t):
                        idx = h * STW + t
                        bb = (2, 3, 7)[idx % 3]
                        po = ps[bb]
                        E.wait("pe", "act", psf[bb]["act"])
                        E.wait("pe", "dve", psf[bb]["dve"])
                        if h == 0:
                            E.wait("pe", "dve", gu_ready["dve"])
                        for i in range(NI):
                            mm(wdb[h % 2][:, i * 128:(i + 1) * 128],
                               gub[t][:, i * TS:(i + 1) * TS], po[:, :],
                               i == 0, i == NI - 1, milestone=(i == NI - 1))
                        if t == n_t - 1:
                            wd_free[h % 2] = dict(E.cnt)
                        E.waitc("dve", "pe")
                        xch = xacc[buf][:, h * STW * TS + t * TS:
                                        h * STW * TS + (t + 1) * TS]
                        dve(lambda xch=xch, po=po: TT(xch, xch, po[:, :],
                                                      ALU.add))
                        psf[bb] = dict(E.cnt)
                        pump(2)
                led_gu_free[0] = dict(E.cnt)
            if stage == "ffn1e1" or stage.startswith("ffn1eK"):
                dump_f32(xacc[0], STW * TS)
                raise _Stop

        def front_pass(st, buf):
            ln_pass(buf, xw_dve=led_gu_free[0]["dve"])
            own = st >= 2
            nch_out = 32 if own else 16
            led6f = [dict(led56[0]), dict(led56[0])]
            E.wait("pe", "dve", led56[0]["dve"])
            for t in range(STW):
                gtile = st * STW + t
                for c in range(nch_out):
                    s_ip = c % 2
                    if c == 0:
                        E.wait("pool", "pe", ip_free[0]["pe"])
                        dma("ipq0", ipwb[0][:], ip_d[0])
                    if c + 1 < nch_out:
                        E.wait("pool", "pe", ip_free[(c + 1) % 2]["pe"])
                        dma(f"ipq{(c + 1) % 2}", ipwb[(c + 1) % 2][:],
                            ip_d[c + 1])
                    E.waitc("pe", f"ipq{s_ip}")
                    pz = ps[c % 4]
                    E.wait("pe", "act", psf[c % 4]["act"])
                    E.wait("pe", "dve", psf[c % 4]["dve"])
                    for k in range(NKD):
                        xnch = xn[:, k * STW * TS + t * TS:
                                  k * STW * TS + (t + 1) * TS]
                        mm(ipwb[s_ip][:, k * 128:(k + 1) * 128], xnch,
                           pz[:, :], k == 0, k == NKD - 1,
                           milestone=(k == NKD - 1))
                    ip_free[s_ip] = dict(E.cnt)
                    if c < 16:
                        if c == 0:
                            E.waitc("dve", "xdq")
                        E.waitc("dve", "act")
                        dve(lambda c=c: ncv.tensor_copy(
                            upw[:, 0:3], ccarry[:, c * 3:(c + 1) * 3]))
                        E.waitc("dve", "dve")
                        E.waitc("act", "dve")
                        E.waitc("act", "pe")
                        act(upw[:, 3:3 + TS], pz[:, :], AF.Copy)
                        psf[c % 4] = dict(E.cnt)
                        E.waitc("dve", "act")
                        dve(lambda c=c: TSC(sq_dv[:], upw[:, 0:TS],
                                            cw4b[:, c * 4:c * 4 + 1], None,
                                            ALU.mult), milestone=False)
                        for j in range(1, 4):
                            dve(lambda c=c, j=j: STT(
                                sq_dv[:], upw[:, j:j + TS],
                                cw4b[:, c * 4 + j:c * 4 + j + 1], sq_dv[:],
                                ALU.mult, ALU.add), milestone=False)
                        dve(lambda c=c: ncv.tensor_copy(
                            ccarry[:, c * 3:(c + 1) * 3], upw[:, TS:TS + 3]))
                        E.waitc("act", "dve")
                        E.waitc("act", "spq")
                        act(ut[:], sq_dv[:], AF.Silu)
                        bk = ps[6 - s_ip]
                        E.waitc("pe", "act")
                        E.wait("pe", "dve", led6f[s_ip]["dve"])
                        E.wait("pe", "act", led6f[s_ip]["act"])
                        mm(xpwb[:, c * 17:(c + 1) * 17], ut[:],
                           bk[0:17, :], True, True, milestone=True)
                        E.waitc("dve", "pe")
                        if c == 0:
                            dve(lambda bk=bk: ncv.tensor_copy(
                                xdbl[:], bk[0:17, :]))
                        else:
                            dve(lambda bk=bk: TT(xdbl[:], xdbl[:],
                                                 bk[0:17, :], ALU.add))
                        led6f[s_ip] = dict(E.cnt)
                        if own:
                            E.waitc("sp", "act")
                            dma("spq", uspill[c * 128:(c + 1) * 128,
                                              (gtile - 4) * TS:
                                              (gtile - 3) * TS], ut[:])
                        if c % 2 == 1:
                            pump(1)
                    else:
                        E.waitc("act", "pe")
                        E.waitc("act", "spq")
                        act(szt[:], pz[:, :], AF.Silu)
                        psf[c % 4] = dict(E.cnt)
                        if c % 2 == 1:
                            pump(1)
                        E.waitc("sp", "act")
                        dma("spq", szspill[(c - 16) * 128:(c - 15) * 128,
                                           (gtile - 4) * TS:(gtile - 3) * TS],
                            szt[:])
                E.waitc("pool", "dve")
                dma("xdq", bspill[:, gtile * TS:(gtile + 1) * TS], xdbl[:])
                front_done[gtile] = dict(E.cnt)
            led56[0] = dict(E.cnt)

        pend = []
        cur_gen = [None]
        brep_rd = [dict(E.cnt)]
        dtb_state = {"rd": dict(E.cnt)}
        dw_rd = [dict(E.cnt) for _ in range(DG)]
        dwadd = [dict(E.cnt) for _ in range(DG)]

        def pump(k=1):
            for _ in range(k):
                if cur_gen[0] is None:
                    if not pend:
                        return
                    cur_gen[0] = scan_units(pend.pop(0))
                try:
                    next(cur_gen[0])
                except StopIteration:
                    cur_gen[0] = None

        def pump_until(tw):
            while tw not in scan_done:
                if cur_gen[0] is None and not pend:
                    raise RuntimeError(f"scan {tw} never scheduled")
                pump(1)

        def scan_units(tw):
            own = tw >= 4
            E.waitc("pool", "xdq")
            dcols = slice(tw * TS, (tw + 1) * TS)
            slot_rd = [dict(E.cnt), dict(E.cnt)]

            E.wait("pool", "dve", dtb_state["rd"]["dve"])
            dma("dtq", sbr[:],
                bspill[0:1, dcols].partition_broadcast(128).squeeze(1))
            E.wait("pool", "dve", brep_rd[0]["dve"])
            for n in range(NSTATE):
                dma("bq0", brep[:, n * TS:(n + 1) * TS],
                    bspill[n + 1:n + 2, dcols]
                    .partition_broadcast(128).squeeze(1))
            bq_full = E.cnt["bq0"]
            for g in range(NG):
                for ci in range(DG):
                    c = g * DG + ci
                    dsl = dw[:, ci * TS:(ci + 1) * TS]
                    E.wait("dve", "act", dw_rd[ci]["act"])
                    E.waitc("dve", "dtq")
                    E.wait("dve", "pool", dw_rd[ci]["pool"])
                    dve(lambda d=dsl, c=c: TSC(d, sbr[:],
                                               dtwcolb[:, c:c + 1], None,
                                               ALU.mult))
                    dtb_state["rd"] = dict(E.cnt)
                    E.waitc("act", "dve")
                    act(sq_sp[:], dsl, AF.Abs, bias=dtbb[:, c:c + 1],
                        milestone=False)
                    act(sq_sp[:], sq_sp[:], AF.Exp, scale=-1.0,
                        milestone=False)
                    act(sq_sp[:], sq_sp[:], AF.Ln, bias=1.0, milestone=False)
                    act(dsl, dsl, AF.Relu, bias=dtbb[:, c:c + 1])
                    E.waitc("dve", "act")
                    dve(lambda d=dsl: TT(d, d, sq_sp[:], ALU.add))
                    dwadd[ci] = dict(E.cnt)
                    yield
                if own:
                    E.waitc("dve", "syq")
                    for ci in range(DG):
                        dve(lambda ci=ci: ncv.memset(
                            sy[:, ci * TS:(ci + 1) * TS], 0.0),
                            milestone=(ci == DG - 1))
                for n in range(NSTATE):
                    bsl = brep[:, n * TS:(n + 1) * TS]
                    for ci in range(DG):
                        c = g * DG + ci
                        slot = (n * DG + ci) % 2
                        dsl = dw[:, ci * TS:(ci + 1) * TS]
                        E.wait("act", "dve", max(slot_rd[slot]["dve"],
                                                 dwadd[ci]["dve"]))
                        act(sa[slot][:], dsl, AF.Exp,
                            scale=acolb[:, c * NSTATE + n:
                                        c * NSTATE + n + 1])
                        exp_led = dict(E.cnt)
                        if g == 0:
                            E.wait("pool", "bq0", bq_full)
                        E.wait("pool", "dve", max(slot_rd[slot]["dve"],
                                                  dwadd[ci]["dve"]))
                        pool(lambda s_=slot, d=dsl, b=bsl: ncp.tensor_tensor(
                            sdb[s_][:], d, b, ALU.mult))
                        db_led = dict(E.cnt)
                        dw_rd[ci] = dict(E.cnt)
                        E.wait("dve", "act", exp_led["act"])
                        E.wait("dve", "pool", db_led["pool"])
                        cc = c * NSTATE + n
                        dve(lambda s_=slot, cc=cc: ncv.tensor_tensor_scan(
                            sh[s_][:], sa[s_][:], sdb[s_][:],
                            carry[:, cc:cc + 1], ALU.mult, ALU.add))
                        E.waitc("dve", "dve")
                        dve(lambda s_=slot, cc=cc: ncv.tensor_copy(
                            carry[:, cc:cc + 1], sh[s_][:, TS - 1:TS]),
                            milestone=False)
                        if own:
                            dve(lambda s_=slot, b=bsl: TT(sdb[s_][:],
                                                          sh[s_][:], b,
                                                          ALU.mult))
                            E.waitc("pool", "dve")
                            pool(lambda ci=ci, s_=slot: ncp.tensor_tensor(
                                sy[:, ci * TS:(ci + 1) * TS],
                                sy[:, ci * TS:(ci + 1) * TS], sdb[s_][:],
                                ALU.add))
                        slot_rd[slot] = dict(E.cnt)
                        yield
                if own:
                    E.waitc("sp", "pool")
                    for ci in range(DG):
                        c = g * DG + ci
                        dma("syq", yspill[c * 128:(c + 1) * 128,
                                          (tw - 4) * TS:(tw - 3) * TS],
                            sy[:, ci * TS:(ci + 1) * TS])
                    yield
            brep_rd[0] = dict(E.cnt)
            scan_done[tw] = dict(E.cnt)

        def p4_gating(tile, buf):
            gt = tile - 4
            pump_until(tile)
            E.waitc("sp", "syq")
            E.waitc("sp", "spq")
            E.waitc("sp", "act")
            E.waitc("sp", "pe")
            rl2 = [(rld_y, rld_u, rld_sz), (sq_ln, ut, szt)]
            for c in range(NCH):
                pump(1)
                ry, ru, rsz = rl2[c % 2]
                E.wait("sp", "dve", rld_free[c % 2]["dve"])
                dma("rlq", ry[:], yspill[c * 128:(c + 1) * 128,
                                         gt * TS:(gt + 1) * TS])
                dma("rlq", ru[:], uspill[c * 128:(c + 1) * 128,
                                         gt * TS:(gt + 1) * TS])
                dma("rlq", rsz[:], szspill[c * 128:(c + 1) * 128,
                                           gt * TS:(gt + 1) * TS])
                E.waitc("dve", "rlq")
                dve(lambda c=c, ru=ru, ry=ry: STT(
                    sq_dv[:], ru[:], dcolb[:, c:c + 1], ry[:],
                    ALU.mult, ALU.add), milestone=False)
                ysl = ygt[:, c * TS:(c + 1) * TS]
                dve(lambda ysl=ysl, rsz=rsz: TT(ysl, sq_dv[:], rsz[:],
                                                ALU.mult))
                rld_free[c % 2] = dict(E.cnt)
            E.waitc("pe", "dve")
            E.wait("pool", "pe", ow_free[0]["pe"])
            dma("owq0", owb[0][:], ow_d[0])
            for h in range(8):
                if h + 1 < 8:
                    E.wait("pool", "pe", ow_free[(h + 1) % 2]["pe"])
                    dma(f"owq{(h + 1) % 2}", owb[(h + 1) % 2][:],
                        ow_d[h + 1])
                E.waitc("pe", f"owq{h % 2}")
                bb = (2, 3, 7)[h % 3]
                po = ps[bb]
                E.wait("pe", "act", psf[bb]["act"])
                E.wait("pe", "dve", psf[bb]["dve"])
                for c in range(NCH):
                    mm(owb[h % 2][:, c * 128:(c + 1) * 128],
                       ygt[:, c * TS:(c + 1) * TS], po[:, :],
                       c == 0, c == NCH - 1, milestone=(c == NCH - 1))
                ow_free[h % 2] = dict(E.cnt)
                E.waitc("dve", "pe")
                xch = xacc[buf][:, h * STW * TS + (tile % STW) * TS:
                                h * STW * TS + (tile % STW + 1) * TS]
                dve(lambda xch=xch, po=po, h=h: STT(
                    xch, po[:, :], obb[:, h:h + 1], xch, ALU.add, ALU.add))
                psf[bb] = dict(E.cnt)

        def dump_f32(buf, ncols):
            E.waitc("sp", "dve")
            E.waitc("sp", "act")
            E.waitc("sp", "pe")
            E.waitc("sp", "pool")
            for ch in range(NKD):
                dma("sx", outT_d[ch * 128:(ch + 1) * 128, 0:ncols],
                    buf[:, ch * ncols:(ch + 1) * ncols])
            fin = E.cnt["sx"]
            for eng in ENGS:
                E.wait(eng, "sx", fin)

        def dump_tile(ap128, col0=0):
            E.waitc("sp", "dve")
            E.waitc("sp", "act")
            E.waitc("sp", "pe")
            E.waitc("sp", "pool")
            dma("sx", outT_d[0:128, col0:col0 + TS], ap128)
            fin = E.cnt["sx"]
            for eng in ENGS:
                E.wait(eng, "sx", fin)

        for st in range(NSTS):
            buf = 0
            if st >= 1:
                fd = front_done[(st - 1) * STW + STW - 1]
                E.wait("sp", "dve", fd["dve"])
                E.wait("sp", "pe", fd["pe"])
                if st >= 3:
                    E.wait("sp", "sx", xspill_done[st - 1]["sx"])
            for ch in range(NKD):
                dma("sx", xacc[buf][:, ch * STW * TS:(ch + 1) * STW * TS],
                    xT_d[ch * 128:(ch + 1) * 128,
                         st * STW * TS:(st + 1) * STW * TS])
            xl = E.cnt["sx"]
            for eng in ("pe", "act", "dve"):
                E.wait(eng, "sx", xl)
            if stage == "skeleton":
                dump_f32(xacc[0], STW * TS)
                return
            ln_pass(buf, router=True)
            if stage == "ln":
                E.waitc("dve", "dve")
                E.waitc("pool", "dve")
                for ch in range(NKD):
                    dma("xdq", outT_d[ch * 128:(ch + 1) * 128, 0:STW * TS],
                        xn[:, ch * STW * TS:(ch + 1) * STW * TS])
                fin = E.cnt["xdq"]
                for eng in ENGS:
                    E.wait(eng, "xdq", fin)
                return
            if stage == "router":
                E.waitc("pool", "act")
                dma("xdq", outT_d[0:8, 0:STW * TS], cwfm[:, :])
                fin = E.cnt["xdq"]
                for eng in ENGS:
                    E.wait(eng, "xdq", fin)
                return
            ffn_pass(buf)
            if stage == "ffn1":
                dump_f32(xacc[0], STW * TS)
                return
            front_pass(st, buf)
            if stage == "front":
                dump_tile(xdbl[0:17, :], 0)
                return
            if st >= 2:
                E.waitc("sp", "dve")
                E.waitc("sp", "pe")
                for ch in range(NKD):
                    dma("sx", xaspill[ch * 128:(ch + 1) * 128,
                                       (st - 2) * STW * TS:
                                       (st - 1) * STW * TS],
                        xacc[buf][:, ch * STW * TS:(ch + 1) * STW * TS])
                xspill_done[st] = dict(E.cnt)
            pend.extend([st * STW, st * STW + 1])

        for ost in range(2):
            st = 2 + ost
            buf = 0
            E.wait("sp", "sx", xspill_done[st]["sx"])
            E.waitc("sp", "dve")
            E.waitc("sp", "pe")
            for ch in range(NKD):
                dma("sx", xacc[buf][:, ch * STW * TS:(ch + 1) * STW * TS],
                    xaspill[ch * 128:(ch + 1) * 128,
                            ost * STW * TS:(ost + 1) * STW * TS])
            rl = E.cnt["sx"]
            for eng in ("pe", "act", "dve"):
                E.wait(eng, "sx", rl)
            for t in range(STW):
                p4_gating(st * STW + t, buf)
            ln_pass(buf, router=True, xw_dve=E.cnt["dve"])
            ffn_pass(buf)
            E.waitc("sp", "dve")
            for ch in range(NKD):
                dma("sx", outT_d[ch * 128:(ch + 1) * 128,
                                  ost * STW * TS:(ost + 1) * STW * TS],
                    xacc[buf][:, ch * STW * TS:(ch + 1) * STW * TS])

        for eng in ENGS:
            for sm in ("sx", "spq", "rlq", "wgq", "wdq0", "wdq1", "ipq0",
                       "ipq1", "owq0", "owq1", "xdq", "syq", "bq0",
                       "dtq"):
                E.waitc(eng, sm)

    handles = dict(pe=nc.tensor, act=nc.scalar, dve=nc.vector,
                   pool=nc.gpsimd, sp=nc.sync)
    E = Emitter(nc, sems, handles)
    finals = {}

    def _run(name):
        E.begin(name)
        try:
            emit_all(E)
        except _Stop:
            pass
        finals[name] = dict(E.cnt)

    @block.sync
    def _(_s):
        _run("sp")

    @block.gpsimd
    def _(_g):
        _run("pool")

    @block.scalar
    def _(_s):
        _run("act")

    @block.vector
    def _(_v):
        _run("dve")

    @block.tensor
    def _(_t):
        _run("pe")

    stack.close()
    vals = list(finals.values())
    for v in vals[1:]:
        assert v == vals[0], f"ledger mismatch: {vals[0]} vs {v}"
    return nc, rem


def _host_stage(inp):
    f = np.ascontiguousarray
    x = np.asarray(inp["L0_attn_x"], dtype=np.float32)
    wg = np.asarray(inp["L0_ffn_wg"], np.float32)
    wu = np.asarray(inp["L0_ffn_wu"], np.float32)
    wd = np.asarray(inp["L0_ffn_wd"], np.float32)
    wgx = np.concatenate([wg, np.asarray(inp["L0_ffn_wg_s"],
                                         np.float32)[None]], 0)
    wux = np.concatenate([wu, np.asarray(inp["L0_ffn_wu_s"],
                                         np.float32)[None]], 0)
    wdx = np.concatenate([wd, np.asarray(inp["L0_ffn_wd_s"],
                                         np.float32)[None]], 0)
    wg_sw = f(wgx.reshape(NE, NKD, 128, FI).transpose(0, 2, 1, 3)
              .reshape(NE, 128, NKD * FI))
    wu_sw = f(wux.reshape(NE, NKD, 128, FI).transpose(0, 2, 1, 3)
              .reshape(NE, 128, NKD * FI))
    wd_sw = f(wdx.reshape(NE, NI, 128, 8, 128).transpose(0, 3, 2, 1, 4)
              .reshape(NE, 8, 128, NI * 128))
    ipT = f(np.asarray(inp["L0_attn_in_proj_w"], np.float32).T)
    ip_sw = f(ipT.reshape(NKD, 128, 32, 128).transpose(2, 1, 0, 3)
              .reshape(32, 128, NKD * 128))
    owT = f(np.asarray(inp["L0_attn_out_w"], np.float32).T)
    ow_sw = f(owT.reshape(NCH, 128, 8, 128).transpose(2, 1, 0, 3)
              .reshape(8, 128, NCH * 128))
    xpT = f(np.asarray(inp["L0_attn_x_proj_w"], np.float32).T)
    xp_sw = f(xpT.reshape(NCH, 128, 17).transpose(1, 0, 2).reshape(128, -1))
    rwT = f(np.asarray(inp["L0_ffn_w_router"], np.float32).T)
    rw_sw = f(rwT.reshape(NKD, 128, 8).transpose(1, 0, 2).reshape(128, -1))
    cw = np.asarray(inp["L0_attn_conv_w"], np.float32)[:, 0, :]
    cw4_sw = f(cw.reshape(NCH, 128, 4).transpose(1, 0, 2).reshape(128, -1))
    dtw_sw = f(np.asarray(inp["L0_attn_dt_w"], np.float32)
               .reshape(NCH, 128).T)
    dtb_sw = f(np.asarray(inp["L0_attn_dt_b"], np.float32)
               .reshape(NCH, 128).T)
    a_sw = f(np.asarray(inp["L0_attn_A_param"], np.float32)
             .reshape(NCH, 128, NSTATE).transpose(1, 0, 2).reshape(128, -1))
    d_sw = f(np.asarray(inp["L0_attn_D_param"], np.float32)
             .reshape(NCH, 128).T)
    ob_sw = f(np.asarray(inp["L0_attn_out_b"], np.float32).reshape(8, 128).T)
    ones_col = np.ones((128, 1), np.float32)
    ones_row = np.ones((1, 128), np.float32)
    id128 = np.eye(128, dtype=np.float32)
    oneh8 = np.zeros((8, 8 * 128), np.float32)
    for e in range(8):
        oneh8[e, e * 128:(e + 1) * 128] = 1.0

    shared = dict(wg_sw=wg_sw, wu_sw=wu_sw, wd_sw=wd_sw, ip_sw=ip_sw,
                  ow_sw=ow_sw, xp_sw=xp_sw, rw_sw=rw_sw, cw4_sw=cw4_sw,
                  dtw_sw=dtw_sw, dtb_sw=dtb_sw, a_sw=a_sw, d_sw=d_sw,
                  ob_sw=ob_sw, ones_col=ones_col, ones_row=ones_row,
                  id128=id128, oneh8=oneh8)
    in_maps = []
    for c in range(NCORES):
        b, role = c >> 1, c & 1
        half0 = x[b, 0:2048]
        if role == 0:
            virt = np.concatenate([np.zeros_like(half0), half0], 0)
        else:
            virt = np.concatenate([half0, x[b, 2048:4096]], 0)
        m = dict(shared)
        m["xT"] = f(virt.T)
        in_maps.append(m)
    return in_maps


def kernel(**inputs):
    in_maps = _host_stage(inputs)
    nc, _rem = build()
    res = run_bass_kernel_spmd(nc, in_maps, core_ids=list(range(NCORES)))
    out = np.empty((4, 4096, 1024), np.float32)
    for c in range(NCORES):
        b, role = c >> 1, c & 1
        o = res.results[c]["outT"]
        out[b, role * 2048:(role + 1) * 2048] = o.T
    return out
